# version 07 (final)
# baseline (speedup 1.0000x reference)
import contextlib
import numpy as np
import ml_dtypes

import concourse.bass as bass
import concourse.mybir as mybir
from concourse.bass_utils import run_bass_kernel_spmd

F32 = mybir.dt.float32
BF16 = mybir.dt.bfloat16
AF = mybir.ActivationFunctionType
ALU = mybir.AluOpType
AX = mybir.AxisListType
bfloat16 = ml_dtypes.bfloat16

NCORES = 8
SEQS = 4
T = 2048
TP = T + 1
D = 1024
DC = 8
H = 8
DH = 64
WIN = 512
NJ = 4
LAT = 256
QK_SCALE = 8.0
HID = 2730
AGP = 2816
ACH = 22
W1N = 2 * AGP
NLAYERS = 2

PB_W = [128, 256, 384, 512, 512, 384, 256, 128]
PB_OFF = {3: 0, 4: 512, 2: 1024, 0: 1024 + 384, 5: 1536, 7: 1536 + 384, 1: 2048, 6: 2048 + 256}
PB_G1 = [3, 4]
PB_G2 = [2, 0, 5, 7, 1, 6]
PB0_W = {4: 512, 5: 384, 6: 256, 7: 128}
PB0_OFF = {4: 0, 5: 512, 7: 896, 6: 1024}
PB0_G1 = [4]
PB0_G2 = [5, 6, 7]


def _rowmap():
    d = np.empty(128, dtype=np.int64)
    for r in range(128):
        if r < 32:
            d[r] = r
        elif r < 64:
            d[r] = r - 32
        elif r < 96:
            d[r] = r - 32
        else:
            d[r] = r - 64
    return d


def _qk_col_perm():
    perm = np.empty(512, dtype=np.int64)
    for p in range(4):
        for r in range(128):
            if r < 32:
                h, dh = 2 * p, r
            elif r < 64:
                h, dh = 2 * p + 1, r - 32
            elif r < 96:
                h, dh = 2 * p, r - 64 + 32
            else:
                h, dh = 2 * p + 1, r - 96 + 32
            perm[p * 128 + r] = h * 64 + dh
    return perm


def _head_rows(h):
    c = h // 2
    if h % 2 == 0:
        return c, (0, 32), (64, 96)
    return c, (32, 64), (96, 128)


def _rotary_tables():
    d_of_r = _rowmap()
    half_idx = np.arange(0, DH, 2, dtype=np.float32)
    inv_freq = (1.0 / (10000.0 ** (half_idx / np.float32(DH)))).astype(np.float32)
    t = np.arange(T, dtype=np.float32)
    freqs_i = (t[:, None] * inv_freq[None, :]).astype(np.float32)
    base = ((half_idx + np.float32(0.4 * DH)) / np.float32(1.4 * DH)).astype(np.float32)
    power = ((t - T // 2) / np.float32(WIN // 2)).astype(np.float32)
    scale_i = (base[None, :] ** power[:, None]).astype(np.float32)
    cos_i = np.cos(freqs_i).astype(np.float32)
    sin_i = np.sin(freqs_i).astype(np.float32)

    Aq = np.empty((128, T), dtype=np.float64)
    Bq = np.empty((128, T), dtype=np.float64)
    Ak = np.empty((128, T), dtype=np.float64)
    Bk = np.empty((128, T), dtype=np.float64)
    for r in range(128):
        i = int(d_of_r[r] % 32)
        s = scale_i[:, i]
        c = cos_i[:, i]
        sn = sin_i[:, i]
        Aq[r] = QK_SCALE * c * s
        Ak[r] = c / s
        rp = (r + 64) % 128
        sgn = -1.0 if rp < 64 else 1.0
        Bq[r] = QK_SCALE * sgn * sn * s
        Bk[r] = sgn * sn / s
    cvt = lambda a: a.astype(np.float32).astype(bfloat16)
    return cvt(Aq), cvt(Bq), cvt(Ak), cvt(Bk)


def _tri_masks():
    kk = np.arange(128)[:, None]
    qq = np.arange(128)[None, :]
    lo = (qq <= kk).astype(np.float32).astype(bfloat16)
    hi = (qq >= kk).astype(np.float32).astype(bfloat16)
    return lo, hi


def _even_rows():
    return np.array([(r < 32) or (64 <= r < 96) for r in range(128)])


def _selectors():
    out = {}
    out["ident"] = np.eye(128, dtype=np.float32).astype(bfloat16)
    statsel = np.zeros((128, 68), dtype=np.float32)
    statsel[:, 0] = 1.0 / 1024.0
    statsel[:, 64] = 1.0 / 1024.0
    out["statsel"] = statsel.astype(bfloat16)
    repa = np.zeros((128, 128), dtype=np.float32)
    repa[0, :] = 1.0
    repa[32, :] = 1.0
    repb = np.zeros((128, 128), dtype=np.float32)
    repb[64, :] = -1.0
    repb[96, :] = -1.0
    out["repsel_a"] = repa.astype(bfloat16)
    out["repsel_b"] = repb.astype(bfloat16)
    even = _even_rows()
    ss = np.zeros((8, 128, 16), dtype=np.float32)
    for role in range(8):
        p = role % 4
        base = 0 if role < 4 else 8
        ss[role, even, base + 2 * p] = 1.0
        ss[role, ~even, base + 2 * p + 1] = 1.0
    out["sumsqsel"] = ss.astype(bfloat16)
    ir = np.zeros((16, 16, 128), dtype=np.float32)
    for hr in range(16):
        ir[hr, hr, :] = 1.0
    out["invsel"] = ir.astype(bfloat16)
    rl = np.zeros((33, 128), dtype=np.float32)
    rl[0, 0:64] = 1.0
    rl[32, 64:128] = 1.0
    out["rowsel"] = rl.astype(bfloat16)
    return out


def _prep_core_inputs(core, inp):
    d = {}
    d["xfm"] = np.ascontiguousarray(
        np.transpose(inp["L0_attn_x"][core * SEQS:(core + 1) * SEQS], (0, 2, 1)))
    d["wth"] = np.ascontiguousarray(inp["L0_attn_W_theta"].T)
    d["wfg"] = np.ascontiguousarray(
        np.concatenate([inp["L0_attn_W_phi"].T, inp["L0_attn_W_g"].T], axis=1))
    perm = _qk_col_perm()
    wqkv = inp["L0_ffn_wqkv"]
    wqk = np.empty((2, 1024, 1024), dtype=np.float32)
    for l in range(2):
        wqk[l][:, 0:512] = wqkv[l][:, 0:512][:, perm]
        wqk[l][:, 512:1024] = wqkv[l][:, 512:1024][:, perm]
    d["wqk"] = np.ascontiguousarray(wqk.reshape(2, 1024, 8, 128).transpose(0, 2, 1, 3))
    d["wv"] = np.ascontiguousarray(wqkv[:, :, 1024:1536])
    d["wo"] = np.ascontiguousarray(
        inp["L0_ffn_wo"].reshape(2, 512, 8, 128).transpose(0, 2, 1, 3))
    w1 = inp["L0_ffn_w1"]
    w1p = np.zeros((2, 1024, W1N), dtype=np.float32)
    w1p[:, :, 0:HID] = w1[:, :, 0:HID]
    w1p[:, :, AGP:AGP + HID] = w1[:, :, HID:]
    d["w1p"] = np.ascontiguousarray(w1p.reshape(2, 1024, 2 * ACH, 128).transpose(0, 2, 1, 3))
    w2p = np.zeros((2, AGP, 1024), dtype=np.float32)
    w2p[:, 0:HID, :] = inp["L0_ffn_w2"]
    d["w2p"] = np.ascontiguousarray(w2p.reshape(2, AGP, DC, 128).transpose(0, 2, 1, 3))
    b1 = inp["L0_ffn_b1"]
    b1p = np.zeros((2, W1N), dtype=np.float32)
    b1p[:, 0:HID] = b1[:, 0:HID]
    b1p[:, AGP:AGP + HID] = b1[:, HID:]
    d["b1r"] = np.ascontiguousarray(b1p.reshape(2, 2 * ACH, 128).transpose(0, 2, 1))
    d["b2r"] = np.ascontiguousarray(inp["L0_ffn_b2"].reshape(2, DC, 128).transpose(0, 2, 1))
    d["lgr"] = np.ascontiguousarray(inp["L0_ffn_ln_g"].reshape(2, DC, 128).transpose(0, 2, 1))
    d["lbr"] = np.ascontiguousarray(inp["L0_ffn_ln_b"].reshape(2, DC, 128).transpose(0, 2, 1))
    dmap = _rowmap()
    d["qsr"] = np.ascontiguousarray(inp["L0_ffn_q_scale"][:, dmap].T)
    d["ksr"] = np.ascontiguousarray(inp["L0_ffn_k_scale"][:, dmap].T)
    return d


def _const_inputs():
    d = {}
    Aq, Bq, Ak, Bk = _rotary_tables()
    d["tAq"], d["tBq"], d["tAk"], d["tBk"] = Aq, Bq, Ak, Bk
    lo, hi = _tri_masks()
    d["trilo"], d["trihi"] = lo, hi
    d.update(_selectors())
    return d


class Sem:
    def __init__(self, nc, stack, name):
        self.h = stack.enter_context(nc.semaphore(name))
        self.c = 0


class Lanes:
    def __init__(self, nc, stack, name, n):
        self.sems = [Sem(nc, stack, f"{name}{i}") for i in range(n)]
        self.i = 0

    def next(self):
        s = self.sems[self.i % len(self.sems)]
        self.i += 1
        return s

    def snapshot(self):
        return [(s, s.c) for s in self.sems if s.c > 0]


class KB:
    def __init__(self, debug=0):
        self.nc = bass.Bass("TRN2", target_bir_lowering=False)
        self.stack = contextlib.ExitStack()
        self.debug = debug
        self.q = {e: [] for e in ("pe", "v", "s", "g", "y")}
        self._waited = {}
        self._mk_io()
        self._mk_sbuf()
        self._mk_psum()
        self._mk_sems()

    TRACE_EMITS = True

    AUTODRAIN = True

    def emit(self, eng, fn, sem=None, k=1, nodrain=False):
        import traceback
        if sem is None:
            g = fn
        else:
            sem.c += k
            h = sem.h
            kk = k
            g = lambda E: fn(E).then_inc(h, kk)
        if self.TRACE_EMITS:
            st = traceback.extract_stack(limit=4)
            g._tb = " | ".join(f"{f.name}:{f.lineno}" for f in st[:-1])
        self.q[eng].append(g)
        if self.AUTODRAIN and not nodrain and eng in ("v", "s", "g"):
            self.q[eng].append(lambda E: E.drain())
        return None if sem is None else sem.c

    def _wait1(self, eng, s, t):
        t = int(t)
        key = (eng, id(s))
        if self._waited.get(key, 0) >= t:
            return
        self._waited[key] = t
        h = s.h
        self.q[eng].append(lambda E, h=h, t=t: E.wait_ge(h, t))

    def wait(self, eng, sem, tok):
        if tok is None:
            return
        if isinstance(tok, tuple):
            tok = [tok]
        if isinstance(tok, list):
            for (s, c) in tok:
                if c > 0:
                    self._wait1(eng, s, c)
            return
        if tok <= 0:
            return
        assert isinstance(sem, Sem), "int token needs a plain Sem"
        self._wait1(eng, sem, tok)

    BARRIERS = True

    def barrier(self):
        if not self.BARRIERS:
            return
        sems = [self.s_pe, self.s_v, self.s_sc, self.s_g]
        for grp in (self.s_wt, self.s_x, self.s_st):
            sems.extend(grp.sems)
        for eng in ("pe", "v", "s", "g", "y"):
            for sm in sems:
                if sm.c > 0:
                    self.wait(eng, sm, sm.c)

    def mm(self, out, lhsT, rhs, start, stop, sem=None, k=1):
        return self.emit("pe", lambda E: E.matmul(out, lhsT, rhs, start=start, stop=stop), sem, k)

    def dma(self, eng, dst, src, sem, k=16):
        if isinstance(sem, Lanes):
            s = sem.next()
            self.emit(eng, lambda E: E.dma_start(dst, src), s, k, nodrain=True)
            return (s, s.c)
        return self.emit(eng, lambda E: E.dma_start(dst, src), sem, k, nodrain=True)

    def finalize(self):
        nc = self.nc
        qq = self.q

        def run_q(E, lst, name):
            for i, f in enumerate(lst):
                try:
                    f(E)
                except Exception:
                    import traceback
                    print(f"FAILED at {name}[{i}]")
                    tb = getattr(f, "_tb", None)
                    if tb:
                        print("emitted at:", tb)
                    raise
        with nc.allow_low_precision(reason="bf16 matmul operands sanctioned for this problem"), \
                nc.Block() as blk:
            @blk.tensor
            def _(E):
                run_q(E, qq["pe"], "pe")

            @blk.vector
            def _(E):
                run_q(E, qq["v"], "v")

            @blk.scalar
            def _(E):
                run_q(E, qq["s"], "s")

            @blk.gpsimd
            def _(E):
                run_q(E, qq["g"], "g")

            @blk.sync
            def _(E):
                run_q(E, qq["y"], "y")
        return nc

    def _mk_io(self):
        nc = self.nc
        ap = lambda name, shape, dt=F32, kind="ExternalInput": \
            nc.dram_tensor(name, list(shape), dt, kind=kind).ap()
        self.i_x = ap("xfm", (SEQS, D, T))
        self.i_wth = ap("wth", (D, LAT))
        self.i_wfg = ap("wfg", (D, LAT + D))
        self.i_wqk = ap("wqk", (2, DC, D, 128))
        self.i_wv = ap("wv", (2, D, 512))
        self.i_wo = ap("wo", (2, DC, 512, 128))
        self.i_w1 = ap("w1p", (2, 2 * ACH, D, 128))
        self.i_w2 = ap("w2p", (2, DC, AGP, 128))
        self.i_b1 = ap("b1r", (2, 128, 2 * ACH))
        self.i_b2 = ap("b2r", (2, 128, DC))
        self.i_lg = ap("lgr", (2, 128, DC))
        self.i_lb = ap("lbr", (2, 128, DC))
        self.i_qs = ap("qsr", (128, 2))
        self.i_ks = ap("ksr", (128, 2))
        self.i_tA = {"q": ap("tAq", (128, T), BF16), "k": ap("tAk", (128, T), BF16)}
        self.i_tB = {"q": ap("tBq", (128, T), BF16), "k": ap("tBk", (128, T), BF16)}
        self.i_trilo = ap("trilo", (128, 128), BF16)
        self.i_trihi = ap("trihi", (128, 128), BF16)
        self.i_ident = ap("ident", (128, 128), BF16)
        self.i_statsel = ap("statsel", (128, 68), BF16)
        self.i_repa = ap("repsel_a", (128, 128), BF16)
        self.i_repb = ap("repsel_b", (128, 128), BF16)
        self.i_sumsqsel = ap("sumsqsel", (8, 128, 16), BF16)
        self.i_invsel = ap("invsel", (16, 16, 128), BF16)
        self.i_rowsel = ap("rowsel", (33, 128), BF16)
        self.o_out = ap("out", (SEQS, D, T), kind="ExternalOutput")
        self.d_w1b = ap("w1b", (2, 2 * ACH, D, 128), BF16, kind="Internal")
        self.d_w2b = ap("w2b", (2, DC, AGP, 128), BF16, kind="Internal")
        self.d_wqkb = ap("wqkb", (2, DC, D, 128), BF16, kind="Internal")
        self.d_wvb = ap("wvb", (2, D, 512), BF16, kind="Internal")
        self.d_wob = ap("wob", (2, DC, 512, 128), BF16, kind="Internal")
        self.d_th = ap("thbuf", (SEQS, 2, 128, T), BF16, kind="Internal")
        self.d_mb = ap("mbbuf", (32, 128), BF16, kind="Internal")
        self.d_mlo = ap("mlobuf", (128, 2 * 1024), BF16, kind="Internal")
        if self.debug:
            self.o_dbg = ap("dbg", (SEQS, D, T), kind="ExternalOutput")
            self.o_dbg2 = ap("dbg2", (SEQS, 2, 128, T), BF16, kind="ExternalOutput")
            self.o_dbg3 = ap("dbg3", (SEQS, 128, 2 * 1024), BF16, kind="ExternalOutput")

    def _mk_sbuf(self):
        nc, st = self.nc, self.stack
        sb = lambda name, shape, dt=F32: st.enter_context(nc.sbuf_tensor(name, list(shape), dt))
        self.x_ffn = sb("x_ffn", (128, DC * TP))
        self.x16r = sb("x16r", (128, 2 * DC * 513), BF16)
        self.x16s = sb("x16s", (128, DC * 513), BF16)
        self.xsq = sb("xsq", (128, 2 * 513), BF16)
        self.tbl = {}
        for t_ in ("Aq", "Bq", "Ak", "Bk"):
            self.tbl[t_] = sb("tb" + t_, (128, WIN), BF16)
        self.m16 = sb("m16", (128, 2 * 1024), BF16)
        self.shA = sb("shA", (128, H * WIN), BF16)
        self.q_aug = self.shA
        self.shB = sb("shB", (128, ACH * WIN), BF16)
        self.k_aug = sb("k_aug", (128, 2 * H * WIN), BF16)
        self.v16 = sb("v16", (128, 2 * 4 * H * 65), BF16)
        self.o16 = sb("o16", (128, 4 * WIN), BF16)
        self.rt1 = sb("rt1", (128, WIN), BF16)
        self.rt2 = sb("rt2", (128, WIN), BF16)
        self.bcast = sb("bcast", (128, WIN), BF16)
        self.invm = self.bcast[0:16, :]
        self.mbar = sb("mbar", (128, 32), BF16)
        self.mbT = sb("mbT", (32, 128), BF16)
        self.aa16 = sb("aa16", (128, 2 * WIN), BF16)
        self.gg16 = sb("gg16", (128, 2 * WIN), BF16)
        self.th_sl = self.aa16
        self.smallf = sb("smallf", (128, 4 * WIN))
        self.denbuf = sb("denbuf", (128, 512), BF16)
        self.stashw = sb("stashw", (128, 16))
        self.den = [self.denbuf[0:1, 0:512], self.denbuf[32:33, 0:512]]
        self.denk = self.denbuf[0:33, 0:512]
        self.drep = sb("drepb", (128, 1024), BF16)
        self.wth16 = sb("wth16", (128, DC * LAT), BF16)
        self.wqks = sb("wqks", (128, 2 * DC * 128), BF16)
        self.wvs = sb("wvs", (128, 2 * 512), BF16)
        self.wos = sb("wos", (128, 2 * 4 * 128), BF16)
        self.w1s = sb("w1s", (128, 4 * DC * 128), BF16)
        self.w2s = sb("w2s", (128, 2 * 11 * 128), BF16)
        self.c_tri = {"lo": sb("c_trilo", (128, 128), BF16), "hi": sb("c_trihi", (128, 128), BF16)}
        self.c_id = sb("c_id", (128, 128), BF16)
        self.c_statsel = sb("c_statsel", (128, 68), BF16)
        self.c_repa = sb("c_repa", (128, 128), BF16)
        self.c_repb = sb("c_repb", (128, 128), BF16)
        self.c_ss = sb("c_ss", (128, 8 * 16), BF16)
        self.c_iv = sb("c_iv", (16, 16 * 128), BF16)
        self.c_rs = sb("c_rs", (33, 128), BF16)
        self.c_b1 = sb("c_b1", (128, 2 * 2 * ACH))
        self.c_b2 = sb("c_b2", (128, 2 * DC))
        self.c_lg = sb("c_lg", (128, 2 * DC))
        self.c_blg = sb("c_blg", (128, 2 * DC))
        self.c_qs = sb("c_qs", (128, 2))
        self.c_ks = sb("c_ks", (128, 2))
        self.c_eps = sb("c_eps", (128, 1))

    def _mk_psum(self):
        nc, st = self.nc, self.stack
        self.pB = st.enter_context(nc.psum_tensor("pB", [128, 2560], F32))
        self.pC = st.enter_context(nc.psum_tensor("pC", [128, 512], F32))
        self.pD = st.enter_context(nc.psum_tensor("pD", [128, 512], F32))
        self.pE = st.enter_context(nc.psum_tensor("pE", [128, 1024], BF16))

    def _mk_sems(self):
        ss = lambda n: Sem(self.nc, self.stack, n)
        ll = lambda n, k: Lanes(self.nc, self.stack, n, k)
        self.s_wt = ll("s_wt", 4)
        self.s_x = ll("s_x", 4)
        self.s_pe = ss("s_pe")
        self.s_v = ss("s_v")
        self.s_sc = ss("s_sc")
        self.s_g = ss("s_g")
        self.s_st = ll("s_st", 2)

    def load_consts(self):
        d = self.dma
        d("y", self.c_tri["lo"][:, :], self.i_trilo, self.s_wt)
        d("y", self.c_tri["hi"][:, :], self.i_trihi, self.s_wt)
        d("y", self.c_id[:, :], self.i_ident, self.s_wt)
        d("y", self.c_statsel[:, :], self.i_statsel, self.s_wt)
        d("y", self.c_repa[:, :], self.i_repa, self.s_wt)
        d("y", self.c_repb[:, :], self.i_repb, self.s_wt)
        d("y", self.c_ss[:, :], self.i_sumsqsel.rearrange("a p c -> p a c"), self.s_wt)
        d("y", self.c_iv[:, :], self.i_invsel.rearrange("a k c -> k a c"), self.s_wt)
        d("y", self.c_rs[:, :], self.i_rowsel, self.s_wt)
        d("y", self.c_b1[:, :], self.i_b1.rearrange("l p c -> p l c"), self.s_wt)
        d("y", self.c_b2[:, :], self.i_b2.rearrange("l p c -> p l c"), self.s_wt)
        d("y", self.c_lg[:, :], self.i_lg.rearrange("l p c -> p l c"), self.s_wt)
        d("y", self.c_qs[:, :], self.i_qs, self.s_wt)
        d("y", self.c_ks[:, :], self.i_ks, self.s_wt)
        t_lb = d("y", self.c_blg[:, :], self.i_lb.rearrange("l p c -> p l c"), self.s_wt)
        shAf = self.shA[:, :].bitcast(F32)
        tw32 = d("y", shAf[:, :], self.i_wth.rearrange("(c p) l -> p c l", p=128), self.s_wt)
        self.wait("g", self.s_wt, tw32)
        tw = self.emit("g", lambda E: E.tensor_copy(self.wth16[:, :], shAf[:, :]), self.s_g)
        self.t_wth_g = tw
        self.t_consts = self.s_wt.snapshot()
        sf = self.smallf
        self.wait("v", self.s_wt, self.t_consts)
        e = self.emit
        e("v", lambda E: E.scalar_tensor_tensor(sf[:, 0:16], self.c_lg[:, :], 1e-30,
                                                self.c_lg[:, :], ALU.bypass, ALU.mult), self.s_v)
        e("v", lambda E: E.tensor_scalar(sf[:, 0:16], sf[:, 0:16], 1e-30, None, ALU.add), self.s_v)
        self.wait("v", self.s_v, self.s_v.c)
        e("v", lambda E: E.reciprocal(sf[:, 16:32], sf[:, 0:16]), self.s_v)
        self.wait("v", self.s_v, self.s_v.c)
        e("v", lambda E: E.tensor_mul(sf[:, 0:16], sf[:, 16:32], self.c_lg[:, :]), self.s_v)
        self.wait("v", self.s_v, self.s_v.c)
        e("v", lambda E: E.tensor_mul(self.c_blg[:, :], self.c_blg[:, :], sf[:, 0:16]), self.s_v)
        self.t_blg = self.s_v.c
        e("v", lambda E: E.memset(self.c_eps[:, :], 1e-5), self.s_v)
        e("v", lambda E: E.memset(self.bcast[:, :], 0.0), self.s_v)
        e("v", lambda E: E.memset(self.smallf[:, :], 0.0), self.s_v)
        e("v", lambda E: E.memset(self.denbuf[:, :], 0.0), self.s_v)
        e("v", lambda E: E.memset(self.drep[:, :], 0.0), self.s_v)
        e("v", lambda E: E.memset(self.mbar[:, :], 0.0), self.s_v)
        for c in range(DC):
            e("v", lambda E, c=c: E.memset(self.x_ffn[:, c * TP:c * TP + 1], 0.0), self.s_v)
        e("v", lambda E: E.memset(self.v16[:, :], 0.0), self.s_v)
        for sl in range(2):
            for tc in range(4):
                for h in range(H):
                    off = ((sl * 4 + tc) * H + h) * 65 + 64
                    e("v", lambda E, off=off: E.memset(self.v16[:, off:off + 1], 1.0), self.s_v)
        e("v", lambda E: E.memset(self.k_aug[64:65, :], 1.0), self.s_v)
        self.barrier()
        self.t_shape = self.s_v.c

    def precast_weights(self):
        cast_tok = [0, 0]
        store_tok = [0, 0]
        slot = 0
        shBf = self.shB[:, :].bitcast(F32)

        def stage(dst_dram, src_dram, cols):
            nonlocal slot
            sl = slot % 2
            fin = shBf[:, sl * 1408:sl * 1408 + cols]
            bout = self.shB[:, 5632 + sl * 1408:5632 + sl * 1408 + cols]
            self.wait("y", self.s_g, cast_tok[sl])
            tok = self.dma("y", fin, src_dram, self.s_wt)
            self.wait("g", self.s_wt, tok)
            self.wait("g", self.s_st, store_tok[sl])
            cast_tok[sl] = self.emit("g", lambda E, bout=bout, fin=fin: E.tensor_copy(
                bout, fin), self.s_g)
            self.wait("y", self.s_g, cast_tok[sl])
            store_tok[sl] = self.dma("y", dst_dram, bout, self.s_st)
            slot += 1

        for l in range(NLAYERS):
            for m in range(2 * ACH):
                stage(self.d_w1b[l, m].rearrange("(c p) n -> p c n", p=128),
                      self.i_w1[l, m].rearrange("(c p) n -> p c n", p=128), 1024)
        for l in range(NLAYERS):
            for dm in range(DC):
                for kk in range(2):
                    stage(self.d_w2b[l, dm, kk * 1408:(kk + 1) * 1408, :]
                          .rearrange("(c p) n -> p c n", p=128),
                          self.i_w2[l, dm, kk * 1408:(kk + 1) * 1408, :]
                          .rearrange("(c p) n -> p c n", p=128), 1408)
        for l in range(NLAYERS):
            for m in range(DC):
                stage(self.d_wqkb[l, m].rearrange("(c p) n -> p c n", p=128),
                      self.i_wqk[l, m].rearrange("(c p) n -> p c n", p=128), 1024)
                stage(self.d_wob[l, m].rearrange("(c p) n -> p c n", p=128),
                      self.i_wo[l, m].rearrange("(c p) n -> p c n", p=128), 512)
            for cc in range(DC):
                stage(self.d_wvb[l, cc * 128:(cc + 1) * 128, :],
                      self.i_wv[l, cc * 128:(cc + 1) * 128, :], 512)
        self.barrier()
        self.t_precast = self.s_st.snapshot()

    def xf(self, c, j, shifted):
        base = c * TP + j * WIN + (0 if shifted else 1)
        return self.x_ffn[:, base:base + WIN]

    def xf_ext(self, c, j):
        base = c * TP + j * WIN
        return self.x_ffn[:, base:base + WIN + 1]

    def x16r_v(self, sl, c, shifted, ext=False):
        base = sl * (DC * 513) + c * 513 + (0 if shifted else 1)
        return self.x16r[:, base:base + (513 if ext else WIN)]

    def x16s_v(self, c, shifted, ext=False):
        base = c * 513 + (0 if shifted else 1)
        return self.x16s[:, base:base + (513 if ext else WIN)]

    def vdrain(self):
        self.emit("v", lambda E: E.drain())

    def sdrain(self):
        self.emit("s", lambda E: E.drain())

    def ln_smalls(self, t_stats_pe):
        self.barrier()
        e, sf, bc = self.emit, self.smallf, self.bcast
        s0 = sf[:, 0:WIN]
        s1 = sf[:, WIN:2 * WIN]
        s2 = sf[:, 2 * WIN:3 * WIN]
        s3 = sf[:, 3 * WIN:4 * WIN]
        self.wait("v", self.s_pe, t_stats_pe)
        e("v", lambda E: E.tensor_copy(s0[0:1, :], self.pC[0:1, :]), self.s_v)
        e("v", lambda E: E.tensor_copy(s1[0:1, :], self.pC[64:65, :]), self.s_v)
        self.vdrain()
        e("v", lambda E: E.tensor_mul(s2[0:1, :], s0[0:1, :], s0[0:1, :]), self.s_v)
        self.vdrain()
        e("v", lambda E: E.scalar_tensor_tensor(s3[0:1, :], s2[0:1, :], -1.0, s1[0:1, :],
                                                ALU.mult, ALU.add), self.s_v)
        t_ve = self.s_v.c
        self.wait("s", self.s_v, t_ve)
        t_sd = self.emit("s", lambda E: E.activation(s1[0:1, :], s3[0:1, :], AF.Sqrt,
                                                     bias=self.c_eps[0:1, 0:1]), self.s_sc)
        self.wait("v", self.s_sc, t_sd)
        e("v", lambda E: E.reciprocal(s2[0:1, :], s1[0:1, :]), self.s_v)
        self.vdrain()
        e("v", lambda E: E.tensor_copy(bc[0:1, :], s2[0:1, :]), self.s_v)
        self.vdrain()
        e("v", lambda E: E.tensor_copy(s3[0:1, :], bc[0:1, :]), self.s_v)
        self.vdrain()
        e("v", lambda E: E.scalar_tensor_tensor(bc[32:33, :], s3[0:1, :], -1.0, s2[0:1, :],
                                                ALU.mult, ALU.add), self.s_v)
        e("v", lambda E: E.tensor_copy(bc[64:65, :], s0[0:1, :]), self.s_v)
        self.vdrain()
        e("v", lambda E: E.tensor_copy(s1[0:1, :], bc[64:65, :]), self.s_v)
        self.vdrain()
        e("v", lambda E: E.scalar_tensor_tensor(bc[96:97, :], s1[0:1, :], -1.0, s0[0:1, :],
                                                ALU.mult, ALU.add), self.s_v)
        self.vdrain()
        self.barrier()
        return self.s_v.c

    def ln_stats(self, sq_in_fn, rhs_x_fn):
        pe_tok = [0] * DC
        sq_tok = [0] * DC
        for c in range(DC):
            self.barrier()
            sl = c % 2
            if c >= 2:
                self.wait("s", self.s_pe, pe_tok[c - 2])
            sq_tok[c] = self.emit("s", lambda E, c=c, sl=sl: E.activation(
                self.xsq[:, sl * 513:sl * 513 + WIN], sq_in_fn(c), AF.Square), self.s_sc)
            self.wait("pe", self.s_sc, sq_tok[c])
            self.mm(self.pC[0:1, :], self.c_statsel[:, 0:1], rhs_x_fn(c),
                    start=(c == 0), stop=(c == DC - 1))
            pe_tok[c] = self.mm(self.pC[64:65, :], self.c_statsel[:, 64:65],
                                self.xsq[:, sl * 513:sl * 513 + WIN],
                                start=(c == 0), stop=(c == DC - 1), sem=self.s_pe)
        return pe_tok[DC - 1]

    def ln_reps(self, t_bcast):
        self.barrier()
        """rep_r -> pD, rep_negmu -> pB[:, 2048:2560]. Returns PE token."""
        self.wait("pe", self.s_v, t_bcast)
        self.mm(self.pD[:, :], self.c_repa[:, :], self.bcast[:, :], start=True, stop=True)
        tok = self.mm(self.pB[:, 2048:2560], self.c_repb[:, :], self.bcast[:, :],
                      start=True, stop=True, sem=self.s_pe)
        return tok

    def seq_outer(self, sq, t_xffn_free):
        self.barrier()
        e = self.emit
        self.wait("y", self.s_v, t_xffn_free)
        for c in range(DC):
            self.dma("y", self.x_ffn[:, c * TP + 1:c * TP + 1 + T],
                     self.i_x[sq, c * 128:(c + 1) * 128, :], self.s_x)
        t_x = self.s_x.snapshot()
        wfg_sb = self.shB[:, 0:DC * (LAT + D)]
        self.barrier()
        shAf = self.shA[:, :].bitcast(F32)
        t_wfg = None
        stg = shAf[:, 0:LAT + D]
        for pc in range(DC):
            if pc > 0:
                self.wait("y", self.s_g, t_wfg)
            tln = self.dma("y", stg, self.i_wfg[pc * 128:(pc + 1) * 128, :], self.s_wt)
            self.wait("g", self.s_wt, tln)
            t_wfg = self.emit("g", lambda E, pc=pc, stg=stg: E.tensor_copy(
                wfg_sb[:, pc * (LAT + D):(pc + 1) * (LAT + D)], stg), self.s_g)
        self.t_wfg_g = t_wfg
        th_stage = self.shB[:, 10 * 1024:10 * 1024 + 2 * WIN]

        phi_sl = self.shA[:, 0:2 * LAT]
        g_sl = self.shA[:, 2 * LAT:2 * LAT + 2 * 1024]

        t_mlast = None
        t_evfg = 0
        for j in range(NJ):
            self.barrier()
            self.wait("g", self.s_pe, self.rd_x16s)
            self.wait("g", self.s_x, t_x)
            t_raw = None
            for c in range(DC):
                t_raw = self.emit("g", lambda E, c=c, j=j: E.tensor_copy(
                    self.x16s[:, c * 512:(c + 1) * 512], self.xf(c, j, False)), self.s_g)
            self.wait("s", self.s_x, t_x)
            self.wait("pe", self.s_g, t_raw)
            self.wait("pe", self.s_v, t_evfg)
            t_stats = self.ln_stats(
                lambda c, j=j: self.xf(c, j, False),
                lambda c: self.x16s[:, c * 512:(c + 1) * 512])
            t_bc = self.ln_smalls(t_stats)
            t_rep = self.ln_reps(t_bc)
            scr = self.smallf[:, 2 * WIN:3 * WIN]
            t_apply = None
            self.barrier()
            for c in range(DC):
                self.wait("v", self.s_pe, t_rep)
                e("v", lambda E, c=c, j=j: E.scalar_tensor_tensor(
                    scr, self.xf(c, j, False), 0.0, self.pB[:, 2048:2560],
                    ALU.add, ALU.add), self.s_v)
                self.vdrain()
                t_apply = self.emit("v", lambda E, c=c, j=j: E.scalar_tensor_tensor(
                    self.xf(c, j, False), scr, 0.0, self.pD[:, :],
                    ALU.add, ALU.mult), self.s_v)
                self.vdrain()
            sl0 = j % 2
            self.wait("g", self.s_v, t_apply)
            t_c16 = None
            for c in range(DC):
                t_c16 = self.emit("g", lambda E, c=c, sl0=sl0, j=j: E.tensor_copy(
                    self.x16r_v(sl0, c, True, ext=True), self.xf_ext(c, j)), self.s_g)
            self.t_x16r = getattr(self, "t_x16r", {})
            self.t_x16r[(0, j)] = t_c16
            self.barrier()
            self.wait("pe", self.s_g, t_c16)
            self.wait("pe", self.s_wt, self.t_consts)
            self.wait("pe", self.s_g, self.t_wth_g)
            for lc in range(2):
                t_th = None
                for c in range(DC):
                    t_th = self.mm(self.pC[:, :] if lc == 0 else self.pD[:, :],
                                   self.wth16[:, c * LAT + lc * 128:c * LAT + lc * 128 + 128],
                                   self.x16r_v(sl0, c, False), start=(c == 0),
                                   stop=(c == DC - 1), sem=self.s_pe)
                self.wait("s", self.s_pe, t_th)
                if j > 0:
                    self.wait("s", self.s_x, t_thd)
                t_ev = self.emit("s", lambda E, lc=lc: E.activation(
                    th_stage[:, lc * WIN:(lc + 1) * WIN],
                    (self.pC if lc == 0 else self.pD)[:, :], AF.Copy), self.s_sc)
            self.wait("y", self.s_sc, t_ev)
            t_thd = self.dma("y", self.d_th[sq, :, :, j * WIN:(j + 1) * WIN]
                             .rearrange("l p n -> p l n"), th_stage, self.s_x)
            self.wait("pe", self.s_g, self.t_wfg_g)
            self.wait("pe", self.s_sc, t_ev)
            for tcl in range(4):
                self.barrier()
                tc = 4 * j + tcl
                pr = tc % 2
                t_fg = None
                for c in range(DC):
                    lhs = self.x16r_v(sl0, c, False)[:, tcl * 128:(tcl + 1) * 128]
                    wrow = wfg_sb[:, c * (LAT + D):(c + 1) * (LAT + D)]
                    self.mm(self.pB[:, 2048:2048 + LAT], lhs, wrow[:, 0:LAT],
                            start=(c == 0), stop=(c == DC - 1))
                    self.mm(self.pC[:, :], lhs, wrow[:, LAT:LAT + 512],
                            start=(c == 0), stop=(c == DC - 1))
                    t_fg = self.mm(self.pD[:, :], lhs, wrow[:, LAT + 512:LAT + 1024],
                                   start=(c == 0), stop=(c == DC - 1), sem=self.s_pe)
                self.wait("v", self.s_pe, t_fg)
                e("v", lambda E, pr=pr: E.tensor_copy(
                    phi_sl[:, pr * LAT:(pr + 1) * LAT], self.pB[:, 2048:2048 + LAT]), self.s_v)
                e("v", lambda E, pr=pr: E.tensor_copy(
                    g_sl[:, pr * 1024:pr * 1024 + 512], self.pC[:, :]), self.s_v)
                t_evfg = self.emit("v", lambda E, pr=pr: E.tensor_copy(
                    g_sl[:, pr * 1024 + 512:pr * 1024 + 1024], self.pD[:, :]), self.s_v)
                self.wait("pe", self.s_v, t_evfg)
                first = (tc == 0)
                for lc in range(2):
                    for nn in range(2):
                        t_mlast = self.mm(
                            self.pB[:, lc * 1024 + nn * 512:lc * 1024 + (nn + 1) * 512],
                            phi_sl[:, pr * LAT + lc * 128:pr * LAT + (lc + 1) * 128],
                            g_sl[:, pr * 1024 + nn * 512:pr * 1024 + (nn + 1) * 512],
                            start=first, stop=(tc == 15), sem=self.s_pe)
        self.barrier()
        self.wait("s", self.s_pe, t_mlast)
        t_m = None
        for lc in range(2):
            for nn in range(2):
                t_m = self.emit("s", lambda E, lc=lc, nn=nn: E.activation(
                    self.m16[:, lc * 1024 + nn * 512:lc * 1024 + (nn + 1) * 512],
                    self.pB[:, lc * 1024 + nn * 512:lc * 1024 + (nn + 1) * 512],
                    AF.Copy, scale=1.0 / 32.0), self.s_sc)
        self.t_m16 = t_m
        self.wait("v", self.s_sc, t_m)
        t_lo = None
        for lc in range(2):
            for nn in range(2):
                sl_ = lc * 1024 + nn * 512
                t_lo = self.emit("v", lambda E, sl_=sl_: E.scalar_tensor_tensor(
                    self.shB[:, sl_:sl_ + 512], self.pB[:, sl_:sl_ + 512], 1.0 / 32.0,
                    self.m16[:, sl_:sl_ + 512], ALU.mult, ALU.subtract), self.s_v)
        self.wait("y", self.s_v, t_lo)
        self.t_mlod = self.dma("y", self.d_mlo, self.shB[:, 0:2048], self.s_x)
        return t_thd


    def cast_x16r(self, l, j):
        self.barrier()
        sl = j % 2
        tok = self.rd_x16r.get(sl, 0)
        self.wait("g", self.s_pe, tok)
        t = None
        for c in range(DC):
            t = self.emit("g", lambda E, c=c, sl=sl, j=j: E.tensor_copy(
                self.x16r_v(sl, c, True, ext=True), self.xf_ext(c, j)), self.s_g)
        self.t_x16r[(l, j)] = t

    def patch_ext_col(self, l, j):
        sl = j % 2
        self.wait("g", self.s_v, self.t_layer_j)
        t = None
        for c in range(DC):
            src_col = self.x_ffn[:, c * TP + j * WIN:c * TP + j * WIN + 1]
            dst_col = self.x16r[:, sl * (DC * 513) + c * 513:
                                sl * (DC * 513) + c * 513 + 1]
            t = self.emit("g", lambda E, src_col=src_col, dst_col=dst_col:
                          E.tensor_copy(dst_col, src_col), self.s_g)
        self.t_x16r[(l, j)] = max(self.t_x16r[(l, j)], t)

    def load_tables(self, l, j):
        self.wait("y", self.s_v, self.rd_tbl_v)
        for nm, srcd in (("Aq", self.i_tA["q"]), ("Bq", self.i_tB["q"]),
                         ("Ak", self.i_tA["k"]), ("Bk", self.i_tB["k"])):
            self.dma("y", self.tbl[nm][:, :],
                     srcd[:, j * WIN:(j + 1) * WIN], self.s_wt)
        self.t_tbl[(l, j)] = self.s_wt.snapshot()

    def load_layer_weights(self, sq, l):
        self.t_wl = getattr(self, "t_precast", 0)

    def qkv_rot(self, sq, l, j, cut=None):
        self.barrier()
        e = self.emit
        sl = j % 2
        tsl = (l * NJ + j) % 2
        self.wait("pe", self.s_wt, self.t_wl)
        self.wait("pe", self.s_g, self.t_x16r[(l, j)])
        self.wait("pe", self.s_sc, self.t_m16)
        qs_ap = self.c_qs[:, l:l + 1]
        ks_ap = self.c_ks[:, l:l + 1]
        slot_v = [0] * 5
        slot_s = [0] * 5
        self.t_ssmm = [0] * 8

        def xr(c, tile=None):
            apv = self.x16r_v(sl, c, c >= 4)
            if tile is None:
                return apv
            return apv[:, tile * 128:(tile + 1) * 128]

        t_war_shA = self.rd_shA
        t_war_kaug = self.rd_kaug.get(sl, 0)

        t_wq = {}
        pe_m = {}
        for m in range(8):
            self.barrier()
            s5 = m % 5
            wsl = m % 2
            wtile = self.wqks[:, wsl * 1024:(wsl + 1) * 1024]
            self.wait("y", self.s_pe, pe_m.get(m - 2, 0))
            self.wait("y", self.s_st, self.t_wl)
            t_wq[m] = self.dma("y", wtile, self.d_wqkb[l, m]
                               .rearrange("(c p) n -> p c n", p=128), self.s_wt)
            if cut == "d":
                pe_m[m] = 0
                continue
            pslot = self.pB[:, s5 * 512:(s5 + 1) * 512]
            self.wait("pe", self.s_sc, slot_s[s5])
            self.wait("pe", self.s_v, slot_v[s5])
            self.wait("pe", self.s_wt, t_wq[m])
            tok = None
            for c in range(DC):
                tok = self.mm(pslot, wtile[:, c * 128:(c + 1) * 128],
                              xr(c), start=(c == 0), stop=(c == DC - 1),
                              sem=self.s_pe if c == DC - 1 else None)
            pe_m[m] = tok
            if cut == "p":
                continue
            isq = m < 4
            A = self.tbl["Aq" if isq else "Ak"][:, :]
            Bp = self.tbl["Bq" if isq else "Bk"][:, :]
            sc_ap = qs_ap if isq else ks_ap
            self.wait("v", self.s_pe, tok)
            self.wait("v", self.s_wt, self.t_tbl[(l, j)])
            e("v", lambda E, pslot=pslot, sc_ap=sc_ap, A=A: E.scalar_tensor_tensor(
                self.rt1[:, :], pslot, sc_ap, A, ALU.mult, ALU.mult), self.s_v)
            e("v", lambda E, pslot=pslot, sc_ap=sc_ap, Bp=Bp: E.scalar_tensor_tensor(
                self.rt2[0:64, :], pslot[64:128, :], sc_ap[64:128, :], Bp[64:128, :],
                ALU.mult, ALU.mult), self.s_v)
            tv = self.emit("v", lambda E, pslot=pslot, sc_ap=sc_ap, Bp=Bp:
                           E.scalar_tensor_tensor(
                               self.rt2[64:128, :], pslot[0:64, :], sc_ap[0:64, :],
                               Bp[0:64, :], ALU.mult, ALU.mult), self.s_v)
            slot_v[s5] = tv
            self.vdrain()
            if cut == "r":
                continue
            self.wait("s", self.s_pe, tok)
            xq = self.xsq[:, (m % 2) * 513:(m % 2) * 513 + WIN]
            if m >= 2:
                self.wait("s", self.s_pe, self.t_ssmm[m - 2])
            scq = self.smallf[:, 0:WIN]
            self.wait("v", self.s_pe, tok)
            tcpq = self.emit("v", lambda E, pslot=pslot, scq=scq: E.tensor_copy(
                scq, pslot), self.s_v)
            self.wait("s", self.s_v, tcpq)
            tsq = self.emit("s", lambda E, scq=scq, xq=xq: E.activation(
                xq, scq, AF.Square), self.s_sc)
            slot_s[s5] = tsq
            if cut in ("t", "u"):
                self.t_ssmm[m] = 0
                continue
            self.wait("pe", self.s_sc, tsq)
            if m == 0:
                self.wait("pe", self.s_v, self.rd_pC_v)
            self.t_ssmm[m] = self.mm(self.pC[0:16, :], self.c_ss[:, m * 16:(m + 1) * 16],
                                     xq, start=(m == 0), stop=(m == 7), sem=self.s_pe)
            if cut == "s":
                continue
            p = m % 4
            hA, hB = 2 * p, 2 * p + 1
            if isq:
                self.wait("v", self.s_pe, t_war_shA)
                blkA = self.shA[:, hA * 512:(hA + 1) * 512]
                blkB = self.shA[:, hB * 512:(hB + 1) * 512]
            else:
                self.wait("v", self.s_pe, t_war_kaug)
                base = sl * (8 * 512)
                blkA = self.k_aug[:, base + hA * 512:base + (hA + 1) * 512]
                blkB = self.k_aug[:, base + hB * 512:base + (hB + 1) * 512]
            for (blk, r0) in ((blkA, 0), (blkB, 32)):
                e("v", lambda E, blk=blk, r0=r0: E.tensor_add(
                    blk[0:32, :], self.rt1[r0:r0 + 32, :], self.rt2[r0:r0 + 32, :]), self.s_v)
                self.emit("v", lambda E, blk=blk, r0=r0: E.tensor_add(
                    blk[32:64, :], self.rt1[r0 + 64:r0 + 96, :],
                    self.rt2[r0 + 64:r0 + 96, :]), self.s_v)

        self.rd_tbl_v = self.s_v.c
        self.barrier()
        if cut in ("m", "d", "p", "r", "s", "t", "u"):
            return
        sf = self.smallf
        self.wait("v", self.s_pe, self.t_ssmm[7])
        e("v", lambda E: E.tensor_copy(sf[0:16, 0:WIN], self.pC[0:16, :]), self.s_v)
        t_cp = self.s_v.c
        self.wait("s", self.s_v, t_cp)
        t_sq2 = self.emit("s", lambda E: E.activation(sf[0:16, WIN:2 * WIN],
                                                      sf[0:16, 0:WIN], AF.Sqrt), self.s_sc)
        self.wait("v", self.s_sc, t_sq2)
        e("v", lambda E: E.tensor_scalar_max(sf[0:16, WIN:2 * WIN],
                                             sf[0:16, WIN:2 * WIN], 1e-12), self.s_v)
        self.vdrain()
        e("v", lambda E: E.reciprocal(sf[0:16, 2 * WIN:3 * WIN], sf[0:16, WIN:2 * WIN]),
          self.s_v)
        self.vdrain()
        t_inv = self.emit("v", lambda E: E.tensor_copy(self.invm[:, :],
                                                       sf[0:16, 2 * WIN:3 * WIN]), self.s_v)
        self.rd_pC_v = self.s_v.c
        self.wait("pe", self.s_v, t_inv)
        t_nrm = [0] * 16
        for hr in range(16):
            self.barrier()
            pp = self.pC if hr % 2 == 0 else self.pD
            if hr >= 2:
                self.wait("pe", self.s_v, t_nrm[hr - 2])
            t_rep = self.mm(pp[:, :], self.c_iv[:, hr * 128:(hr + 1) * 128],
                            self.invm[:, :], start=True, stop=True, sem=self.s_pe)
            h = hr % 8
            if hr < 8:
                blk = self.shA[:, h * 512:(h + 1) * 512]
            else:
                blk = self.k_aug[:, sl * (8 * 512) + h * 512:sl * (8 * 512) + (h + 1) * 512]
            self.wait("v", self.s_pe, t_rep)
            t_nrm[hr] = self.emit("v", lambda E, blk=blk, pp=pp: E.tensor_mul(
                blk[0:64, :], blk[0:64, :], pp[0:64, :]), self.s_v)
        self.t_qkaug = t_nrm[15]

        self.barrier()
        if cut == "n":
            self.rd_x16r[sl] = self.s_pe.c
            self.t_v16 = 0
            return
        for s5 in range(4):
            self.wait("pe", self.s_sc, slot_s[s5 % 5] if s5 < 5 else 0)
            self.wait("pe", self.s_v, slot_v[s5 % 5] if s5 < 5 else 0)
        t_vload = {}
        tok = None
        for c in range(DC):
            self.barrier()
            wsl = c % 2
            wv_sl = self.wvs[:, wsl * 512:(wsl + 1) * 512]
            self.wait("y", self.s_pe, t_vload.get(c - 2, 0))
            t_wv = self.dma("y", wv_sl, self.d_wvb[l, c * 128:(c + 1) * 128, :], self.s_wt)
            self.wait("pe", self.s_wt, t_wv)
            for tcl in range(4):
                tok = self.mm(self.pB[:, tcl * 512:(tcl + 1) * 512], xr(c, tcl), wv_sl,
                              start=(c == 0), stop=(c == DC - 1),
                              sem=self.s_pe if tcl == 3 else None)
            t_vload[c] = tok
        t_vdone = tok
        self.wait("v", self.s_pe, t_vdone)
        self.wait("v", self.s_pe, self.rd_v16.get(sl, 0))
        t_vp = None
        for tcl in range(4):
            base = (sl * 4 + tcl) * (H * 65)
            dst = self.v16[:, base:base + H * 65].rearrange(
                "p (h x) -> p h x", h=H)[:, :, 0:64]
            pslot = self.pB[:, tcl * 512:(tcl + 1) * 512]
            t_vp = self.emit("v", lambda E, dst=dst, pslot=pslot: E.tensor_copy(
                dst, pslot.rearrange("p (h d) -> p h d", h=H)), self.s_v)
        self.t_v16 = t_vp
        self.barrier()
        self.rd_x16r[sl] = self.s_pe.c

    def attn(self, sq, l, j):
        self.barrier()
        e = self.emit
        sl_cur = j % 2
        sl_prev = (j - 1) % 2
        kbase = lambda slot, h: slot * (8 * 512) + h * 512

        self.wait("pe", self.s_v, self.t_qkaug)
        self.wait("pe", self.s_v, self.t_v16)
        t_max = {}
        first_max = True
        for h in range(H):
            for qt in range(NJ):
                self.barrier()
                idx = h * 4 + qt
                soff = 0 if idx % 2 == 0 else 1024
                scr = self.pB[:, soff:soff + 640]
                if idx >= 2:
                    self.wait("pe", self.s_v, t_max[idx - 2])
                wA = 512 - 128 * qt
                qtile = self.shA[0:64, h * 512 + qt * 128:h * 512 + (qt + 1) * 128]
                if j > 0 and wA > 0:
                    self.mm(scr[:, 0:wA], qtile,
                            self.k_aug[0:64, kbase(sl_prev, h) + 128 * qt:
                                       kbase(sl_prev, h) + 512],
                            start=True, stop=True)
                if qt > 0:
                    self.mm(scr[:, wA:512], qtile,
                            self.k_aug[0:64, kbase(sl_cur, h):kbase(sl_cur, h) + 128 * qt],
                            start=True, stop=True)
                tok = self.mm(scr[:, 512:640], qtile,
                              self.k_aug[0:64, kbase(sl_cur, h) + 128 * qt:
                                         kbase(sl_cur, h) + 128 * qt + 128],
                              start=True, stop=True, sem=self.s_pe)
                a0 = 0 if j > 0 else wA
                self.wait("v", self.s_pe, tok)
                if first_max:
                    self.wait("v", self.s_pe, self.rd_mbar)
                    first_max = False
                t_max[idx] = self.emit("v", lambda E, scr=scr, a0=a0, idx=idx:
                                       E.tensor_reduce(self.mbar[:, idx:idx + 1],
                                                       scr[:, a0:640], AX.X, ALU.max,
                                                       negate=True), self.s_v)
        self.barrier()
        self.wait("pe", self.s_v, t_max[31])
        t_tr = self.emit("pe", lambda E: E.transpose(
            self.pE[0:32, 640:768], self.mbar[:, 0:32], self.c_id[:, :]), self.s_pe)
        self.rd_mbar = t_tr
        self.wait("v", self.s_pe, t_tr)
        t_mc = self.emit("v", lambda E: E.tensor_copy(self.mbT[:, :],
                                                      self.pE[0:32, 640:768]), self.s_v)
        self.wait("y", self.s_v, t_mc)
        t_mb1 = self.dma("y", self.d_mb, self.mbT[:, :], self.s_wt)
        self.wait("y", None, t_mb1)
        t_scat = self.dma("y", self.shA[64:65, 0:4096],
                          self.d_mb.rearrange("a b -> (a b)")[None, :], self.s_wt)

        if j == 0:
            widths, offs = PB0_W, PB0_OFF
            g1, g2 = PB0_G1, PB0_G2
            clist = [4, 5, 6, 7]
            w1c, w2c = 512, 1280
        else:
            widths, offs = dict(enumerate(PB_W)), PB_OFF
            g1, g2 = PB_G1, PB_G2
            clist = list(range(8))
            w1c, w2c = 1024, 2560
        t_exp = {}
        t_av = {}
        t_evac = {}
        t_rp_pair = {}
        self.wait("pe", self.s_wt, t_scat)

        def span(c):
            if j == 0:
                return 128 * (c - 4), 512
            return max(0, 128 * c - 512), min(512, 128 * c + 128)

        for h in range(H):
            self.barrier()
            pt = self.shB[:, (h % 2) * 2560:(h % 2) * 2560 + 2560]
            self.wait("s", self.s_pe, t_av.get(h - 2, 0))

            def piece_mm(c, grp_wait):
                slot = sl_prev if c < 4 else sl_cur
                lo, hi = span(c)
                lhs = self.k_aug[0:65, kbase(slot, h) + (c % 4) * 128:
                                 kbase(slot, h) + (c % 4) * 128 + 128]
                rhs = self.shA[0:65, h * 512 + lo:h * 512 + hi]
                self.wait("pe", self.s_sc, grp_wait)
                return self.mm(self.pB[:, offs[c]:offs[c] + widths[c]], lhs, rhs,
                               start=True, stop=True, sem=self.s_pe)

            for c in g1:
                tg1 = piece_mm(c, t_exp.get((h - 1, 1), 0))
            for c in g2:
                tg2 = piece_mm(c, t_exp.get((h - 1, 2), 0))
            sfx = self.smallf
            self.wait("v", self.s_pe, tg1)
            tcp1 = self.emit("v", lambda E, w1c=w1c, sfx=sfx: E.tensor_copy(
                sfx[:, 0:w1c], self.pB[:, 0:w1c]), self.s_v)
            self.wait("s", self.s_v, tcp1)
            t_exp[(h, 1)] = self.emit("s", lambda E, pt=pt, w1c=w1c, sfx=sfx:
                                      E.activation(pt[:, 0:w1c], sfx[:, 0:w1c],
                                                   AF.Exp), self.s_sc)
            self.wait("v", self.s_pe, tg2)
            self.wait("v", self.s_sc, t_exp[(h, 1)])
            tcp2 = self.emit("v", lambda E, w1c=w1c, w2c=w2c, sfx=sfx: E.tensor_copy(
                sfx[:, 0:w2c - w1c], self.pB[:, w1c:w2c]), self.s_v)
            self.wait("s", self.s_v, tcp2)
            t_exp[(h, 2)] = self.emit("s", lambda E, pt=pt, w1c=w1c, w2c=w2c, sfx=sfx:
                                      E.activation(pt[:, w1c:w2c], sfx[:, 0:w2c - w1c],
                                                   AF.Exp), self.s_sc)
            self.wait("v", self.s_sc, t_exp[(h, 2)])
            tt = None
            for c in clist:
                w = widths[c]
                if j > 0 and c <= 3:
                    off = offs[c] + w - 128
                    msk = self.c_tri["lo"]
                elif c >= 4:
                    off = offs[c]
                    msk = self.c_tri["hi"]
                else:
                    continue
                tt = self.emit("v", lambda E, pt=pt, off=off, msk=msk: E.tensor_mul(
                    pt[:, off:off + 128], pt[:, off:off + 128], msk[:, :]), self.s_v)
            pav = self.pC if h % 2 == 0 else self.pD
            self.wait("pe", self.s_v, tt)
            self.wait("pe", self.s_v, t_evac.get(h - 2, 0))
            if h == 0:
                self.wait("pe", self.s_v, self.rd_pC_v)
            n_writes = sum((span(c)[1] - span(c)[0]) // 128 for c in clist)
            w_i = 0
            for c in clist:
                lo, hi = span(c)
                slot = sl_prev if c < 4 else sl_cur
                vb = (slot * 4 + (c % 4)) * (H * 65) + h * 65
                for s in range(lo // 128, hi // 128):
                    w_i += 1
                    t_av[h] = self.mm(pav[0:65, s * 128:(s + 1) * 128],
                                      self.v16[:, vb:vb + 65],
                                      pt[:, offs[c] + s * 128 - lo:
                                         offs[c] + s * 128 - lo + 128],
                                      start=(w_i == 1),
                                      stop=(w_i == n_writes), sem=self.s_pe)
            self.wait("s", self.s_pe, t_av[h])
            self.wait("s", self.s_pe, t_rp_pair.get(h - 2, 0))
            t_den = self.emit("s", lambda E, pav=pav, h=h: E.copy(
                self.den[h % 2][0:1, :], pav[64:65, 0:512]), self.s_sc)
            if h % 2 == 1:
                pE32 = self.pE[:, :].bitcast(F32)
                self.wait("pe", self.s_sc, t_den)
                t_rp = self.mm(pE32[:, 0:512], self.c_rs[:, :], self.denk,
                               start=True, stop=True, sem=self.s_pe)
                t_rp_pair[h] = t_rp
                t_rp_pair[h - 1] = t_rp
                dre = self.drep[:, 0:WIN]
                dro = self.drep[:, WIN:2 * WIN]
                self.wait("v", self.s_pe, t_rp)
                e("v", lambda E, dre=dre, pE32=pE32: E.reciprocal(
                    dre[0:64, :], pE32[0:64, 0:512]), self.s_v)
                e("v", lambda E, dro=dro, pE32=pE32: E.reciprocal(
                    dro[0:64, :], pE32[64:128, 0:512]), self.s_v)
                self.vdrain()
                c2 = h // 2
                if c2 == 0:
                    self.wait("v", self.s_pe, self.rd_o16)
                e("v", lambda E, c2=c2, dre=dre: E.scalar_tensor_tensor(
                    self.o16[0:64, c2 * 512:(c2 + 1) * 512], self.pC[0:64, 0:512], 0.0,
                    dre[0:64, :], ALU.add, ALU.mult), self.s_v)
                t_evac[h] = self.emit("v", lambda E, c2=c2, dro=dro: E.scalar_tensor_tensor(
                    self.o16[64:128, c2 * 512:(c2 + 1) * 512], self.pD[0:64, 0:512], 0.0,
                    dro[0:64, :], ALU.add, ALU.mult), self.s_v)
                t_evac[h - 1] = t_evac[h]
        self.barrier()
        self.t_o16 = t_evac[7]
        self.rd_v16[sl_prev] = self.s_pe.c
        self.rd_kaug[sl_prev] = self.s_pe.c
        self.t_attn_pe = self.s_pe.c

    def wo_resid(self, sq, l, j):
        t_ev = [0] * 8
        t_wload = {}
        tok = None
        for dm in range(DC):
            self.barrier()
            s5 = dm % 2
            wsl = dm % 2
            wo_sl = self.wos[:, wsl * 512:(wsl + 1) * 512]
            self.wait("y", self.s_pe, t_wload.get(dm - 2, 0))
            t_wo_l = self.dma("y", wo_sl, self.d_wob[l, dm]
                              .rearrange("(c p) n -> p c n", p=128), self.s_wt)
            pslot = self.pB[:, s5 * 512:(s5 + 1) * 512]
            self.wait("pe", self.s_v, t_ev[dm - 2] if dm >= 2 else self.t_o16)
            self.wait("pe", self.s_wt, t_wo_l)
            for c2 in range(4):
                tok = self.mm(pslot, wo_sl[:, c2 * 128:(c2 + 1) * 128],
                              self.o16[:, c2 * 512:(c2 + 1) * 512],
                              start=(c2 == 0), stop=(c2 == 3),
                              sem=self.s_pe if c2 == 3 else None)
            t_wload[dm] = tok
            self.wait("v", self.s_pe, tok)
            t_ev[dm] = self.emit("v", lambda E, pslot=pslot, dm=dm, j=j: E.scalar_tensor_tensor(
                self.xf(dm, j, False), pslot, 0.0, self.xf(dm, j, False),
                ALU.add, ALU.add), self.s_v)
        self.barrier()
        for c in range(4, DC):
            self.emit("v", lambda E, c=c, j=j: E.tensor_copy(
                self.stashw[:, (j % 2) * 4 + c - 4:(j % 2) * 4 + c - 3],
                self.x_ffn[:, c * TP + (j + 1) * WIN:c * TP + (j + 1) * WIN + 1]),
                self.s_v)
        self.barrier()
        self.t_wo = t_ev[7]
        self.rd_o16 = tok

    def ln2_hhat(self, sq, l, j):
        self.barrier()
        e = self.emit
        if j > 0:
            for c in range(4, DC):
                e("v", lambda E, c=c, j=j: E.tensor_copy(
                    self.stashw[:, 8 + c - 4:8 + c - 3],
                    self.x_ffn[:, c * TP + j * WIN:c * TP + j * WIN + 1]), self.s_v)
            self.barrier()
            for c in range(4, DC):
                e("v", lambda E, c=c, j=j: E.tensor_copy(
                    self.x_ffn[:, c * TP + j * WIN:c * TP + j * WIN + 1],
                    self.stashw[:, ((j + 1) % 2) * 4 + c - 4:
                                ((j + 1) % 2) * 4 + c - 3]), self.s_v)
            self.barrier()
        self.wait("g", self.s_v, self.t_wo)
        self.wait("g", self.s_pe, self.rd_x16s)
        t_c = None
        for c in range(DC):
            t_c = self.emit("g", lambda E, c=c, j=j: E.tensor_copy(
                self.x16s_v(c, True, ext=True), self.xf_ext(c, j)), self.s_g)
        self.wait("pe", self.s_g, t_c)
        self.wait("pe", self.s_v, self.t_o16)
        t_stats = self.ln_stats(
            lambda c, j=j: self.xf(c, j, c >= 4),
            lambda c: self.x16s_v(c, c >= 4))
        self.rd_x16s = t_stats
        t_bc = self.ln_smalls(t_stats)
        t_rep = self.ln_reps(t_bc)
        scr = self.smallf[:, 2 * WIN:3 * WIN]
        t_h = None
        self.barrier()
        for c in range(DC):
            self.wait("v", self.s_pe, t_rep)
            e("v", lambda E, c=c, j=j, l=l: E.scalar_tensor_tensor(
                scr, self.xf(c, j, c >= 4), self.c_blg[:, l * DC + c:l * DC + c + 1],
                self.pB[:, 2048:2560], ALU.add, ALU.add), self.s_v)
            self.vdrain()
            t_h = self.emit("v", lambda E, c=c, l=l: E.scalar_tensor_tensor(
                self.shA[:, c * 512:(c + 1) * 512], scr,
                self.c_lg[:, l * DC + c:l * DC + c + 1], self.pD[:, :],
                ALU.mult, ALU.mult), self.s_v)
            self.vdrain()
        self.barrier()
        if j > 0:
            for c in range(4, DC):
                e("v", lambda E, c=c, j=j: E.tensor_copy(
                    self.x_ffn[:, c * TP + j * WIN:c * TP + j * WIN + 1],
                    self.stashw[:, 8 + c - 4:8 + c - 3]), self.s_v)
            self.barrier()
        self.t_hhat = t_h

    def mlp(self, sq, l, j):
        e = self.emit
        t_mult_v = {}
        t_mult_s = {}
        t_cps = {}
        t_w1tok = {}
        for mp in range(ACH):
            self.barrier()
            p2 = mp % 2
            sa = self.w1s[:, (p2 * 2) * 1024:(p2 * 2 + 1) * 1024]
            sg = self.w1s[:, (p2 * 2 + 1) * 1024:(p2 * 2 + 2) * 1024]
            self.wait("y", self.s_pe, t_w1tok.get(mp - 2, 0))
            ta = self.dma("y", sa, self.d_w1b[l, mp].rearrange("(c p) n -> p c n", p=128),
                          self.s_wt)
            tb = self.dma("y", sg, self.d_w1b[l, mp + ACH]
                          .rearrange("(c p) n -> p c n", p=128), self.s_wt)
            pa = self.pB[:, (p2 * 2) * 512:(p2 * 2 + 1) * 512]
            pg = self.pB[:, (p2 * 2 + 1) * 512:(p2 * 2 + 2) * 512]
            self.wait("pe", None, [ta, tb])
            if mp == 0:
                self.wait("pe", self.s_v, self.t_hhat)
            self.wait("pe", self.s_v, t_cps.get(mp - 2, 0))
            tok = None
            for c in range(DC):
                self.mm(pa, sa[:, c * 128:(c + 1) * 128],
                        self.shA[:, c * 512:(c + 1) * 512],
                        start=(c == 0), stop=(c == DC - 1))
                tok = self.mm(pg, sg[:, c * 128:(c + 1) * 128],
                              self.shA[:, c * 512:(c + 1) * 512],
                              start=(c == 0), stop=(c == DC - 1),
                              sem=self.s_pe if c == DC - 1 else None)
            t_w1tok[mp] = tok
            aslot = self.aa16[:, p2 * WIN:(p2 + 1) * WIN]
            gslot = self.gg16[:, p2 * WIN:(p2 + 1) * WIN]
            sfa = self.smallf[:, p2 * 1024:p2 * 1024 + WIN]
            sfg = self.smallf[:, p2 * 1024 + WIN:p2 * 1024 + 2 * WIN]
            self.wait("v", self.s_pe, tok)
            self.wait("v", self.s_sc, t_mult_s.get(mp - 2, 0))
            e("v", lambda E, pa=pa, sfa=sfa: E.tensor_copy(sfa, pa), self.s_v)
            t_cpg = self.emit("v", lambda E, pg=pg, sfg=sfg: E.tensor_copy(sfg, pg),
                              self.s_v)
            self.wait("s", self.s_v, t_cpg)
            self.wait("s", self.s_v, t_mult_v.get(mp - 2, 0))
            e("s", lambda E, aslot=aslot, sfa=sfa, mp=mp, l=l: E.add(
                aslot, sfa, self.c_b1[:, l * 2 * ACH + mp:l * 2 * ACH + mp + 1]), self.s_sc)
            t_g = self.emit("s", lambda E, gslot=gslot, sfg=sfg, mp=mp, l=l: E.activation(
                gslot, sfg, AF.Gelu,
                bias=self.c_b1[:, l * 2 * ACH + ACH + mp:l * 2 * ACH + ACH + mp + 1]),
                self.s_sc)
            t_mult_s[mp] = t_g
            t_cps[mp] = t_cpg
            self.wait("v", self.s_sc, t_g)
            if mp == 0:
                self.wait("v", self.s_pe, self.t_attn_pe)
            t_mult_v[mp] = self.emit("v", lambda E, aslot=aslot, gslot=gslot, mp=mp:
                                     E.tensor_mul(self.shB[:, mp * 512:(mp + 1) * 512],
                                                  aslot, gslot), self.s_v)
        self.rd_shA = t_w1tok[ACH - 1]
        self.barrier()
        t_w2ev = {}
        t_half = {}
        for dm in range(DC):
            self.barrier()
            p2 = dm % 2
            pslot = self.pB[:, (4 - p2) * 512:(5 - p2) * 512]
            self.wait("pe", self.s_v, t_w2ev.get(dm - 2, 0))
            if dm == 0:
                self.wait("pe", self.s_v, t_cps[ACH - 1])
                self.wait("pe", self.s_v, t_mult_v[ACH - 1])
            tok = None
            for hh in range(2):
                hsl = (2 * dm + hh) % 2
                ws = self.w2s[:, hsl * 11 * 128:(hsl + 1) * 11 * 128]
                self.wait("y", self.s_pe, t_half.get(2 * dm + hh - 2, 0))
                tw = self.dma("y", ws, self.d_w2b[l, dm, hh * 1408:(hh + 1) * 1408, :]
                              .rearrange("(c p) n -> p c n", p=128), self.s_wt)
                self.wait("pe", self.s_wt, tw)
                for kk in range(11):
                    k = hh * 11 + kk
                    tok = self.mm(pslot, ws[:, kk * 128:(kk + 1) * 128],
                                  self.shB[:, k * 512:(k + 1) * 512],
                                  start=(k == 0), stop=(k == ACH - 1),
                                  sem=self.s_pe if k == ACH - 1 else None)
                t_half[2 * dm + hh] = tok if tok is not None else self.s_pe.c
            self.wait("v", self.s_pe, tok)
            t_w2ev[dm] = self.emit("v", lambda E, pslot=pslot, dm=dm, j=j, l=l: E.scalar_tensor_tensor(
                self.xf(dm, j, False), pslot, self.c_b2[:, l * DC + dm:l * DC + dm + 1],
                self.xf(dm, j, False), ALU.add, ALU.add), self.s_v)
        self.barrier()
        self.t_layer_j = t_w2ev[7]

    def final(self, sq):
        self.barrier()
        e = self.emit
        t_sil = [0] * DC
        for c in range(DC):
            self.wait("s", self.s_v, self.t_layer_j)
            t_sil[c] = self.emit("s", lambda E, c=c: E.activation(
                self.x_ffn[:, c * TP + 1:c * TP + 1 + T],
                self.x_ffn[:, c * TP + 1:c * TP + 1 + T], AF.Silu), self.s_sc)
        outv = self.shA[:, :].bitcast(F32)
        xrel = self.smallf
        t_mlo_l = self.dma("y", self.shB[:, 0:2048], self.d_mlo, self.s_x)
        self.wait("y", self.s_sc, self.s_sc.c)
        self.wait("y", self.s_v, self.s_v.c)
        t_cb = {}
        t_acc = {}
        t_mmtok = {}
        for jt in range(NJ):
            self.barrier()
            if jt > 0:
                self.wait("y", self.s_pe, t_mmtok[(jt - 1, DC - 1)])
            t_th = self.dma("y", self.th_sl[:, :],
                            self.d_th[sq, :, :, jt * WIN:(jt + 1) * WIN]
                            .rearrange("l p n -> p l n"), self.s_x)
            for c in range(DC):
                self.barrier()
                s5 = c % 4
                pslot = self.pB[:, s5 * 512:(s5 + 1) * 512]
                self.wait("pe", self.s_x, t_th)
                self.wait("pe", self.s_sc, self.t_m16)
                self.wait("pe", self.s_v, t_cb.get((jt, c - 4), t_cb.get((jt - 1, c + 4), 0)))
                self.wait("pe", self.s_x, t_mlo_l)
                tok = None
                for lc in range(2):
                    self.mm(pslot, self.m16[:, lc * 1024 + c * 128:
                                            lc * 1024 + (c + 1) * 128],
                            self.th_sl[:, lc * WIN:(lc + 1) * WIN],
                            start=(lc == 0), stop=False)
                    tok = self.mm(pslot, self.shB[:, lc * 1024 + c * 128:
                                                  lc * 1024 + (c + 1) * 128],
                                  self.th_sl[:, lc * WIN:(lc + 1) * WIN],
                                  start=False, stop=(lc == 1),
                                  sem=self.s_pe if lc == 1 else None)
                t_mmtok[(jt, c)] = tok
                xsl = xrel[:, (c % 2) * WIN:(c % 2) * WIN + WIN]
                self.wait("y", self.s_v, t_cb.get((jt, c - 2), t_cb.get((jt - 1, c + 6), 0)))
                t_xr = self.dma("y", xsl, self.i_x[sq, c * 128:(c + 1) * 128,
                                                   jt * WIN:(jt + 1) * WIN], self.s_x)
                self.wait("v", self.s_pe, tok)
                self.wait("v", self.s_sc, t_sil[c])
                if jt == 0 and c == 0:
                    self.wait("v", self.s_pe, self.rd_shA)
                prev = t_acc.get((jt, c - 4), t_acc.get((jt - 1, c + 4), 0))
                self.wait("v", self.s_st, prev)
                ov = outv[:, s5 * 512:(s5 + 1) * 512]
                e("v", lambda E, pslot=pslot, c=c, jt=jt, ov=ov:
                  E.scalar_tensor_tensor(
                      ov, pslot, 0.0,
                      self.x_ffn[:, c * TP + 1 + jt * WIN:
                                 c * TP + 1 + (jt + 1) * WIN],
                      ALU.add, ALU.mult), self.s_v)
                self.wait("v", self.s_x, t_xr)
                t_cb[(jt, c)] = self.emit("v", lambda E, ov=ov, xsl=xsl:
                                          E.tensor_add(ov, ov, xsl), self.s_v)
                self.wait("y", self.s_v, t_cb[(jt, c)])
                t_acc[(jt, c)] = self.dma("y", self.o_out[sq, c * 128:(c + 1) * 128,
                                                          jt * WIN:(jt + 1) * WIN],
                                          ov, self.s_st)
        self.barrier()
        self.t_final_v = self.s_v.c
        self.t_final_g = self.s_st.snapshot()

    def build(self, upto="full"):
        self.t_x16r = {}
        self.t_tbl = {}
        self.rd_x16r = {}
        self.rd_tbl = {}
        self.rd_v16 = {}
        self.rd_kaug = {}
        self.rd_wqk = 0
        self.rd_x16s = 0
        self.rd_mbar = 0
        self.rd_tbl_v = 0
        self.rd_o16 = 0
        self.rd_pC_v = 0
        self.rd_shA = 0
        self.t_m16 = 0
        self.t_attn_pe = 0
        self.t_hhat = 0
        self.load_consts()
        if upto == "consts":
            self.wait("y", None, self.s_st.snapshot())
            self.wait("y", None, self.s_x.snapshot())
            self.wait("y", None, self.s_wt.snapshot())
            return self.finalize()
        if upto != "outer":
            self.precast_weights()
        if upto == "precast":
            self.wait("y", None, self.s_st.snapshot())
            self.wait("y", None, self.s_x.snapshot())
            self.wait("y", None, self.s_wt.snapshot())
            return self.finalize()
        if upto in ("q0", "qd", "qp", "qr", "qs", "qt", "qu", "qm", "qn", "q1", "a1", "w1", "m1"):
            self.seq_outer(0, 0)
            self.load_layer_weights(0, 0)
            self.cast_x16r(0, 0)
            self.load_tables(0, 0)
            if upto == "q0":
                self.barrier()
                self.wait("y", None, self.s_st.snapshot())
                self.wait("y", None, self.s_x.snapshot())
                self.wait("y", None, self.s_wt.snapshot())
                return self.finalize()
            self.qkv_rot(0, 0, 0, cut={"qm": "m", "qn": "n", "qd": "d", "qp": "p", "qr": "r",
                                       "qs": "s", "qt": "t", "qu": "u"}.get(upto))
            if upto in ("a1", "w1", "m1"):
                self.attn(0, 0, 0)
            if upto in ("w1", "m1"):
                self.wo_resid(0, 0, 0)
            if upto == "m1":
                self.ln2_hhat(0, 0, 0)
                self.mlp(0, 0, 0)
            self.barrier()
            self.wait("y", None, self.s_st.snapshot())
            self.wait("y", None, self.s_x.snapshot())
            self.wait("y", None, self.s_wt.snapshot())
            return self.finalize()
        t_xffn_free = 0
        t_shA_free = 0
        for sq in range(SEQS):
            self.seq_outer(sq, t_xffn_free)
            if upto == "outer":
                if self.debug:
                    self.dump_dbg(sq)
                t_xffn_free = self.s_v.c
                continue
            for l in range(NLAYERS):
                self.load_layer_weights(sq, l)
                if l == 0:
                    self.cast_x16r(0, 0)
                for j in range(NJ):
                    if j < NJ - 1:
                        self.cast_x16r(l, j + 1)
                    elif l + 1 < NLAYERS:
                        self.cast_x16r(l + 1, 0)
                    self.load_tables(l, j)
                    self.qkv_rot(sq, l, j)
                    self.attn(sq, l, j)
                    self.wo_resid(sq, l, j)
                    if upto == "l0wo" and l == 0:
                        continue
                    self.ln2_hhat(sq, l, j)
                    self.mlp(sq, l, j)
                self.rd_wqk = self.s_pe.c
                if upto == "l0" and l == 0:
                    break
            if self.debug:
                self.dump_dbg(sq)
            if upto in ("l0", "l0wo"):
                t_xffn_free = self.s_v.c
                continue
            self.final(sq)
            t_xffn_free = self.t_final_v
        self.wait("y", None, self.s_st.snapshot())
        self.wait("y", None, self.s_x.snapshot())
        self.wait("y", None, self.s_wt.snapshot())
        return self.finalize()

    def dump_dbg(self, sq):
        self.wait("y", self.s_v, self.s_v.c)
        self.wait("y", self.s_sc, self.s_sc.c)
        self.wait("y", None, self.s_x.snapshot())
        for c in range(DC):
            self.dma("y", self.o_dbg[sq, c * 128:(c + 1) * 128, :],
                     self.x_ffn[:, c * TP + 1:c * TP + 1 + T], self.s_st)
        self.dma("y", self.o_dbg2[sq].rearrange("l p n -> p l n"),
                 self.d_th[sq].rearrange("l p n -> p l n"), self.s_st)
        self.dma("y", self.o_dbg3[sq], self.m16[:, :], self.s_st)


_PROG_CACHE = {}


def _get_prog(debug=0, upto="full"):
    key = (debug, upto)
    if key not in _PROG_CACHE:
        kb = KB(debug=debug)
        nc = kb.build(upto=upto)
        _PROG_CACHE[key] = (nc, kb)
    return _PROG_CACHE[key]


def kernel(**inputs):
    nc, _ = _get_prog(debug=0, upto="full")
    consts = _const_inputs()
    in_maps = []
    for core in range(NCORES):
        d = _prep_core_inputs(core, inputs)
        d.update(consts)
        in_maps.append(d)
    res = run_bass_kernel_spmd(nc, in_maps, core_ids=list(range(NCORES)))
    out = np.empty((32, T, D), dtype=np.float32)
    for core in range(NCORES):
        o = res.results[core]["out"]
        out[core * SEQS:(core + 1) * SEQS] = np.transpose(o, (0, 2, 1))
    return out
